# version 18 (final)
# baseline (speedup 1.0000x reference)
import math
from contextlib import ExitStack

import numpy as np
import ml_dtypes

import concourse.bass as bass
import concourse.mybir as mybir
from concourse.bass_utils import run_bass_kernel_spmd

BF16 = mybir.dt.bfloat16
F32 = mybir.dt.float32
AF = mybir.ActivationFunctionType
ALU = mybir.AluOpType
BF = ml_dtypes.bfloat16


class Cfg:
    def __init__(self, S=8192, D=2048, H=8192, NC=8):
        assert H == 4 * D
        self.S, self.D, self.H, self.NC = S, D, H, NC
        self.T = S // NC
        self.TC = self.T // 128
        self.TH = self.T // 2
        self.THC = self.TC // 2
        self.DC = D // 128
        self.HC = H // 128
        self.ET = max(1, D // 512)
        self.EW = min(512, D)
        self.DSL = D // NC
        assert self.DSL % 128 == 0
        self.DPC = self.DSL // 128
        self.AQ = 16 if self.HC % 16 == 0 else self.HC
        self.a_exp = int(round(math.log2(self.HC) / 2))
        self.b_exp = 4
        self.fix = (2.0 ** (self.a_exp + self.b_exp)) / math.sqrt(H)
        pw = 1024
        while 8 * pw + 8 * self.DSL > self.TC * D:
            pw //= 2
        assert pw >= 128
        self.PW = pw
        self.NPC = 128 * self.DSL // pw


class Ctr:
    __slots__ = ("name", "n", "h")

    def __init__(self, name):
        self.name, self.n, self.h = name, 0, None


class Bank:
    __slots__ = ("idx", "t", "writer", "readers")

    def __init__(self, idx, t):
        self.idx, self.t = idx, t
        self.writer = None
        self.readers = {}


class Prog:

    def __init__(self):
        self.events = []
        self.ctrs = []
        self.chain = {}

    def ctr(self, name):
        c = Ctr(name)
        self.ctrs.append(c)
        return c

    def ev(self, eng, fn, waits=(), incs=(), chain=True):
        ics_in = list(incs)
        ch = self.chain.get(eng) if chain else None
        if ch is not None and not any(c is ch for (c, _) in ics_in):
            ics_in.append((ch, 1))
        ws = [(c, v) for (c, v) in waits if v > 0]
        out = {}
        ics = []
        for (c, n) in ics_in:
            c.n += n
            ics.append((c, n))
            out[c] = c.n
        self.events.append((eng, ws, fn, ics, chain))
        return out

    def check(self):
        order = ["sync", "gpsimd", "tensor", "vector", "scalar"]
        qs = {e: [ev for ev in self.events if ev[0] == e] for e in order}
        pos = {e: 0 for e in order}
        cnt = {c: 0 for c in self.ctrs}
        progressed = True
        while progressed:
            progressed = False
            for e in order:
                while pos[e] < len(qs[e]):
                    (_, ws, fn, ics, _ch) = qs[e][pos[e]]
                    if any(cnt[c] < v for (c, v) in ws):
                        break
                    for (c, n) in ics:
                        cnt[c] += n
                    pos[e] += 1
                    progressed = True
        stuck = {e: (len(qs[e]) - pos[e]) for e in order if pos[e] < len(qs[e])}
        if stuck:
            lines = []
            for e in stuck:
                (_, ws, fn, ics, _ch) = qs[e][pos[e]]
                bad = [(c.name, v, cnt[c]) for (c, v) in ws if cnt[c] < v]
                lines.append(f"{e}@{pos[e]}: needs {bad}")
            raise RuntimeError("deadlock in event graph: " + "; ".join(lines))

    def realize(self, nc, ctx):
        self.check()
        for c in self.ctrs:
            c.h = ctx.enter_context(nc.semaphore(c.name))
        block = ctx.enter_context(nc.Block())
        order = ["sync", "gpsimd", "tensor", "vector", "scalar"]
        evs_by = {e: [] for e in order}
        for (eng, ws, fn, ics, ch) in self.events:
            evs_by[eng].append((ws, fn, ics, ch))

        def make(evs, ch):
            def body(eng):
                seen = {}
                k = 0
                for (ws, fn, ics, chained) in evs:
                    if ch is not None and chained and k > 0:
                        eng.wait_ge(ch.h, k)
                        seen[ch] = k
                    for (c, v) in ws:
                        if seen.get(c, 0) < v:
                            eng.wait_ge(c.h, v)
                            seen[c] = v
                    inst = fn(eng)
                    for (c, n) in ics:
                        inst = inst.then_inc(c.h, n)
                    if ch is not None and chained:
                        k += 1
            return body

        for e in order:
            getattr(block, e)(make(evs_by[e], self.chain.get(e)))


def build(cfg: Cfg):
    nc = bass.Bass("TRN2", target_bir_lowering=False)
    S, D, H, NC = cfg.S, cfg.D, cfg.H, cfg.NC
    T, TC, TH, THC = cfg.T, cfg.TC, cfg.TH, cfg.THC
    DC, HC, ET, EW = cfg.DC, cfg.HC, cfg.ET, cfg.EW
    DSL, DPC, AQ = cfg.DSL, cfg.DPC, cfg.AQ
    PW, NPC = cfg.PW, cfg.NPC

    ein = lambda n, s, dt: nc.dram_tensor(n, s, dt, kind="ExternalInput")
    x_tok = ein("x_tok", [T, D], F32)
    wqkvg = {p: ein(f"w{p}s", [ET, 128, DC, EW], F32) for p in "kvqg"}
    bias4 = ein("bias4", [4, D], F32)
    w1s = ein("w1s", [HC, 128, DC, 128], F32)
    w3s = ein("w3s", [HC, 128, DC, 128], F32)
    w2ts = ein("w2ts", [HC, 128 * DSL], F32)
    u128 = ein("u128", [128, 128], BF16)
    identb = ein("identb", [128, 128], BF16)
    ha = ein("ha", [HC, HC], BF16)
    hb = ein("hb", [128, 128], BF16)
    onesr = ein("onesr", [1, 128], BF16)
    onesc = ein("onesc", [128, 1], BF16)
    pmask = ein("pmask", [4, 4 * 128], BF16)
    rmask = ein("rmask", [NC, 1], BF16)
    out_tok = nc.dram_tensor("out_tok", [T, D], F32, kind="ExternalOutput")

    w13b = nc.dram_tensor("w13b", [2, HC, 128, DC * 128], BF16)
    w2b = nc.dram_tensor("w2b", [HC, 128 * DSL], BF16)
    w2f_in = nc.dram_tensor("w2f_in", [H, DSL], BF16)
    w2f_g = nc.dram_tensor("w2f_g", [NC * H, DSL], BF16, addr_space="Shared")
    cc_in = nc.dram_tensor("cc_in", [1, D], F32)
    cc_out = nc.dram_tensor("cc_out", [NC, D], F32)

    prog = Prog()
    PE = prog.ctr("pe")
    VE = prog.ctr("ve")
    SE = prog.ctr("se")
    prog.chain = {"vector": VE, "scalar": SE}
    CC1 = prog.ctr("cc1")
    CC2 = prog.ctr("cc2")
    FIN = prog.ctr("fin")
    CST = prog.ctr("cst")
    GB4 = prog.ctr("gb4")
    GP = [prog.ctr(f"gp{i}") for i in range(4)]
    SB = [prog.ctr(f"sb{i}") for i in range(4)]
    SA = [prog.ctr(f"sa{i}") for i in range(4)]
    SWF = [prog.ctr(f"swf{i}") for i in range(4)]
    GQ = [prog.ctr("gq0"), prog.ctr("gq1")]
    GL = [prog.ctr(f"gl{i}") for i in range(4)]
    GW13 = prog.ctr("gw13")
    SWS = [prog.ctr("sws0"), prog.ctr("sws1")]
    GG = prog.ctr("gg")
    CIN = prog.ctr("cin")
    SXs = [prog.ctr(f"sx{i}") for i in range(5)]

    ctx = ExitStack()
    sb = lambda n, s, dt: ctx.enter_context(nc.sbuf_tensor(n, s, dt))

    xnT = sb("xnT", [128, DC * T], BF16)
    hid = sb("hid", [128, 2 * TC * D], BF16)
    kv_of = 0
    q_of = TC * D
    g_s = sb("g_s", [128, TC * D], BF16)
    wbuf = sb("wbuf", [128, 8192], BF16)
    aux = sb("aux", [128, 8192], BF16)
    b4 = sb("b4", [4, D], BF16)
    pm = sb("pm", [4, 4 * 128], BF16)
    gat = sb("gat", [NC, D], BF16)
    st = [sb("st1", [128, 512], BF16), sb("st2", [128, 512], BF16)]
    xn_st = sb("xn_st", [128, 2 * D], BF16)
    pfx_bf = sb("pfx_bf", [1, D], BF16)
    u_sb = sb("u_sb", [128, 128], BF16)
    id_sb = sb("id_sb", [128, 128], BF16)
    ha_sb = sb("ha_sb", [HC, HC], BF16)
    hb_sb = sb("hb_sb", [128, 128], BF16)
    or_sb = sb("or_sb", [1, 128], BF16)
    oc_sb = sb("oc_sb", [128, 1], BF16)
    rm_sb = sb("rm_sb", [NC, 1], BF16)
    x_st = sb("x_st", [128, D], F32)
    o_st = sb("o_st", [128, D], F32)
    pfx = o_st[0:1, :]
    rows = sb("rows", [128, 16], F32)

    banks = [Bank(i, ctx.enter_context(nc.psum_tensor(f"pb{i}", [128, 512], F32)))
             for i in range(8)]
    st_rd = [None, None]

    def dma(queue, dst, src, c, waits=()):
        return prog.ev(queue, lambda e, d=dst, s=src: e.dma_start(out=d, in_=s),
                       waits=waits, incs=[(c, 16)], chain=False)

    def mm(lhsT, rhs, bank, view, start=True, stop=True, waits=()):
        ws = list(waits)
        if start:
            for c, v in bank.readers.items():
                ws.append((c, v))
            bank.readers = {}
        r = prog.ev("tensor",
                    lambda e, o=view, l=lhsT, rr=rhs, s1=start, s2=stop:
                        e.matmul(o, l, rr, start=s1, stop=s2),
                    waits=ws, incs=[(PE, 1)])
        if stop:
            bank.writer = (PE, r[PE])
        return r

    def evac(eng_name, c, fn, bank, waits=()):
        ws = list(waits)
        if bank.writer is not None:
            ws.append(bank.writer)
        r = prog.ev(eng_name, fn, waits=ws, incs=[(c, 1)])
        bank.readers[c] = r[c]
        return r

    def transpose_grp(src_aps, bank, waits=()):
        views, r = [], None
        for j, sap in enumerate(src_aps):
            view = bank.t[:, j * 64:(j + 1) * 64].bitcast(BF16)
            ws = []
            if j == 0:
                ws = list(waits)
                for c, v in bank.readers.items():
                    ws.append((c, v))
                bank.readers = {}
            r = prog.ev("tensor",
                        lambda e, o=view, s=sap: e.transpose(o, s, id_sb[:, :]),
                        waits=ws, incs=[(PE, 1)])
            views.append(view)
        bank.writer = (PE, r[PE])
        return views

    dma("sync", u_sb[:, :], u128[:, :], CST)
    dma("sync", id_sb[:, :], identb[:, :], CST)
    dma("sync", ha_sb[:, :], ha[:, :], CST)
    dma("sync", hb_sb[:, :], hb[:, :], CST)
    dma("sync", or_sb[:, :], onesr[:, :], CST)
    dma("sync", oc_sb[:, :], onesc[:, :], CST)
    dma("sync", rm_sb[:, :], rmask[:, :], CST)
    dma("sync", pm[:, :], pmask[:, :], CST)
    consts_ld = (CST, CST.n)
    dma("gpsimd", b4[:, :], bias4[:, :], GB4)
    bias_gw = (GB4, GB4.n)

    LEAD = 3
    NPS = NAS = LEAD + 1
    p_in = [g_s[:, i * PW:(i + 1) * PW] for i in range(NPS)]
    p_out = [g_s[:, (NPS + i) * PW:(NPS + i + 1) * PW] for i in range(NPS)]
    a_base = 2 * NPS * PW
    a_in = [g_s[:, a_base + i * DSL:a_base + (i + 1) * DSL] for i in range(NAS)]
    a_out = [g_s[:, a_base + (NAS + i) * DSL:a_base + (NAS + i + 1) * DSL]
             for i in range(NAS)]
    prep_state = dict(piece_cons=[None] * NPS, w2b_w=[None] * NPS,
                      a_cons=[None] * NAS, w2f_w=[None] * NAS, all_w2b=None,
                      ldv={})
    w2b_r = w2b.ap().rearrange("a (b d) -> a b d", b=128)
    w2f_in_r = w2f_in.ap().rearrange("(b a) d -> b a d", a=HC)
    BN = min(512, PW)
    NSTEPS = NPC + HC
    prep_front = [0]
    prep_back = [0]

    def prep_load(i):
        stt = prep_state
        if i < NPC:
            sl = i % NPS
            ws = [stt["piece_cons"][sl]] if stt["piece_cons"][sl] else []
            ld = dma("gpsimd", p_in[sl][0:HC, :], w2ts[:, i * PW:(i + 1) * PW],
                     GP[sl], waits=ws)
            stt["ldv"][i] = (GP[sl], ld[GP[sl]])
        else:
            al = i - NPC
            sl = al % NAS
            ws = list(stt["all_w2b"])
            if stt["a_cons"][sl]:
                ws.append(stt["a_cons"][sl])
            ld = dma("sync", a_in[sl][:, :], w2b_r[al, :, :], SA[sl], waits=ws)
            stt["ldv"][i] = (SA[sl], ld[SA[sl]])

    def prep_compute(i):
        stt = prep_state
        if i < NPC:
            sl = i % NPS
            ev = r = None
            for j in range(PW // BN):
                bk = banks[6 + (j % 2)]
                r = mm(ha_sb[:, :], p_in[sl][0:HC, j * BN:(j + 1) * BN],
                       bk, bk.t[0:HC, 0:BN],
                       waits=[stt["ldv"][i], consts_ld])
                ev = evac("scalar", SE,
                          lambda e, o=p_out[sl][0:HC, j * BN:(j + 1) * BN],
                                 s=bk.t[0:HC, 0:BN]: e.activation(o, s, AF.Identity),
                          bk, waits=[stt["w2b_w"][sl]]
                          if (j == 0 and stt["w2b_w"][sl]) else [])
            stt["piece_cons"][sl] = (PE, r[PE])
            wd = dma("scalar", w2b[:, i * PW:(i + 1) * PW], p_out[sl][0:HC, :],
                     SB[sl], waits=[(SE, ev[SE])])
            stt["w2b_w"][sl] = (SB[sl], wd[SB[sl]])
            if i == NPC - 1:
                stt["all_w2b"] = [(s, s.n) for s in SB]
        else:
            al = i - NPC
            sl = al % NAS
            bk = banks[6 + sl % 2]
            r = mm(hb_sb[:, :], a_in[sl][:, :], bk, bk.t[:, 0:DSL],
                   waits=[stt["ldv"][i], consts_ld])
            stt["a_cons"][sl] = (PE, r[PE])
            ev = evac("scalar", SE,
                      lambda e, o=a_out[sl][:, :], s=bk.t[:, 0:DSL]:
                          e.activation(o, s, AF.Identity, scale=float(cfg.fix)),
                      bk, waits=[stt["w2f_w"][sl]] if stt["w2f_w"][sl] else [])
            wd = dma("scalar", w2f_in_r[:, al, :], a_out[sl][:, :], SWF[sl],
                     waits=[(SE, ev[SE])])
            stt["w2f_w"][sl] = (SWF[sl], wd[SWF[sl]])

    def prep_tick():
        f, b = prep_front[0], prep_back[0]
        if f < NSTEPS and not (f >= NPC and b < NPC):
            prep_load(f)
            prep_front[0] = f = f + 1
        if b < f and (f - b > LEAD or f == NSTEPS or (f == NPC and b < NPC)):
            prep_compute(b)
            prep_back[0] = b + 1

    def prep_drain():
        while prep_back[0] < NSTEPS:
            prep_tick()

    qbufs = [wbuf[:, 0:DC * EW], aux[:, 0:DC * EW]]
    ve_a_end = [0]
    qcons = [[], []]
    kv_et_done = [None] * ET
    q_done, g_done = {}, {}
    xn_done = [None] * TC
    prep_done_waits = None

    def emit_qkvg_tc(pi, pname, et, sl, wl, tc, nbank=2):
        bk = banks[2 + tc % nbank]
        for dcc in range(DC):
            mm(xnT[:, dcc * T + tc * 128:dcc * T + (tc + 1) * 128],
               qbufs[sl][:, dcc * EW:(dcc + 1) * EW],
               bk, bk.t[:, 0:EW], start=(dcc == 0), stop=False,
               waits=([(GQ[sl], wl[GQ[sl]]), xn_done[tc]] if dcc == 0 else []))
        mm(pm[:, pi * 128:(pi + 1) * 128], b4[:, et * EW:(et + 1) * EW],
           bk, bk.t[:, 0:EW], start=False, stop=True, waits=[bias_gw])
        if pname == "k":
            evac("scalar", SE,
                 lambda e, o=hid[:, kv_of + tc * D + et * EW:
                                 kv_of + tc * D + (et + 1) * EW],
                        s=bk.t[:, 0:EW]: e.activation(o, s, AF.Identity), bk)
        elif pname == "v":
            kvv = hid[:, kv_of + tc * D + et * EW:kv_of + tc * D + (et + 1) * EW]
            ev = evac("vector", VE,
                      lambda e, o=kvv, s=bk.t[:, 0:EW]: e.tensor_mul(o, o, s),
                      bk, waits=[(SE, SE.n)])
            if tc == TC - 1:
                kv_et_done[et] = (VE, ev[VE])
        elif pname == "q":
            ev = evac("scalar", SE,
                      lambda e, o=hid[:, q_of + tc * D + et * EW:
                                      q_of + tc * D + (et + 1) * EW],
                             s=bk.t[:, 0:EW]: e.activation(o, s, AF.Identity),
                      bk, waits=[(VE, max(s2_last, ve_a_end[0]))])
            q_done[(tc, et)] = (SE, ev[SE])
        else:
            ev = evac("scalar", SE,
                      lambda e, o=g_s[:, tc * D + et * EW:tc * D + (et + 1) * EW],
                             s=bk.t[:, 0:EW]: e.activation(o, s, AF.Sigmoid),
                      bk, waits=prep_done_waits)
            g_done[(tc, et)] = (SE, ev[SE])

    for _ in range(4):
        prep_tick()
    wl0 = dma("gpsimd", qbufs[0][:, :].rearrange("p (c e) -> p c e", c=DC),
              wqkvg["k"][0, :, :, :], GQ[0])
    wl1 = dma("gpsimd", qbufs[1][:, :].rearrange("p (c e) -> p c e", c=DC),
              wqkvg["k"][min(1, ET - 1), :, :, :], GQ[1])
    r_eps = rows[:, 15:16]
    ev0 = prog.ev("vector", lambda e: e.memset(r_eps, 1e-5), incs=[(VE, 1)])
    eps_set = ev0[VE]
    se_v2_prev = [0, 0]
    nq = max(0, min(3, (TC - 2) // 2))
    xbufs = [x_st, o_st] + [
        hid[:, q_of + 2 * j * D:q_of + (2 * j + 2) * D].bitcast(F32)
        for j in range(nq)]
    NXB = len(xbufs)
    x_users = [[consts_ld] for _ in range(NXB)]
    s2scr = hid[:, q_of + (TC - 1) * D:q_of + TC * D]
    prev_tp = [0, 0]
    row_rd = [0, 0]
    s2_last = 0
    for tc in range(TC):
        par = tc % 2
        xbi = tc % NXB
        r_s1, r_s2 = rows[:, par * 7:par * 7 + 1], rows[:, par * 7 + 1:par * 7 + 2]
        r_m = rows[:, par * 7 + 2:par * 7 + 3]
        r_v = rows[:, par * 7 + 3:par * 7 + 4]
        r_v2 = rows[:, par * 7 + 4:par * 7 + 5]
        r_std = rows[:, par * 7 + 5:par * 7 + 6]
        r_rst = rows[:, par * 7 + 6:par * 7 + 7]
        xn_v = xn_st[:, par * D:(par + 1) * D]
        xb = xbufs[xbi]
        SX = SXs[xbi]
        if tc >= 2:
            emit_qkvg_tc(0, "k", 0, 0, wl0, tc - 2, nbank=4)
            if ET > 1:
                emit_qkvg_tc(0, "k", 1, 1, wl1, tc - 2, nbank=4)
        xl = dma("sync", xb[:, :], x_tok[tc * 128:(tc + 1) * 128, :], SX,
                 waits=x_users[xbi])
        sq = prog.ev("scalar",
                     lambda e, o=xn_v, xx=xb, r=r_s1: e.activation(
                         o, xx[:, :], AF.Identity, accum_out=r),
                     waits=[(SX, xl[SX]), (PE, prev_tp[par]), (VE, row_rd[par])],
                     incs=[(SE, 1)])
        s2 = prog.ev("vector",
                     lambda e, xx=xb, r=r_s2: e.scalar_tensor_tensor(
                         s2scr[:, :], xx[:, :], 1.0, xx[:, :],
                         op0=ALU.mult, op1=ALU.mult, accum_out=r),
                     waits=[(SX, xl[SX]), (SE, se_v2_prev[par])], incs=[(VE, 1)])
        s2_last = s2[VE]
        prog.ev("scalar",
                lambda e, a=r_m, b=r_s1: e.activation(a, b, AF.Identity,
                                                      scale=1.0 / D),
                waits=[(VE, row_rd[par])], incs=[(SE, 1)])
        prog.ev("scalar", lambda e, a=r_v, b=r_m: e.activation(a, b, AF.Square))
        prog.ev("scalar",
                lambda e, a=r_v: e.activation(a, a, AF.Identity, scale=-1.0,
                                              bias=r_eps),
                waits=[(VE, eps_set)])
        v2ev = prog.ev("scalar",
                       lambda e, a=r_v2, b=r_s2, c=r_v: e.activation(
                           a, b, AF.Identity, scale=1.0 / D, bias=c),
                       waits=[(VE, s2[VE])], incs=[(SE, 1)])
        se_v2_prev[par] = v2ev[SE]
        prog.ev("scalar", lambda e, a=r_std, b=r_v2: e.activation(a, b, AF.Ln))
        rc = prog.ev("scalar",
                     lambda e, a=r_rst, b=r_std: e.activation(a, b, AF.Exp,
                                                              scale=-0.5),
                     incs=[(SE, 1)])
        ap = prog.ev("vector",
                     lambda e, o=xn_v, xx=xb, a=r_m, b=r_rst: e.tensor_scalar(
                         o, xx[:, :], a, b,
                         op0=ALU.subtract, op1=ALU.mult),
                     waits=[(SE, rc[SE])], incs=[(VE, 1)])
        row_rd[par] = ap[VE]
        x_users[xbi] = [(VE, ap[VE]), (VE, s2[VE]), (SE, sq[SE])]
        ev = None
        xnT_r = xnT[:, :].rearrange("p (c t) -> p c t", t=T)
        for grp in range(DC // 4):
            bk = banks[grp % 2]
            views = transpose_grp(
                [xn_v[:, (grp * 4 + j) * 128:(grp * 4 + j + 1) * 128]
                 for j in range(4)],
                bk, waits=[(VE, ap[VE]), consts_ld])
            dst = xnT_r[:, grp * 4:(grp + 1) * 4, tc * 128:(tc + 1) * 128]
            srcv = bk.t[:, 0:256].bitcast(BF16).rearrange(
                "p (c t) -> p c t", t=128)
            ev = evac("vector", VE,
                      lambda e, o=dst, s=srcv: e.tensor_copy(o, s), bk)
        prev_tp[par] = PE.n
        xn_done[tc] = (VE, ev[VE])
        prep_tick()
        prep_tick()
    for tcl in (TC - 2, TC - 1):
        emit_qkvg_tc(0, "k", 0, 0, wl0, tcl, nbank=4)
        if ET > 1:
            emit_qkvg_tc(0, "k", 1, 1, wl1, tcl, nbank=4)
    qcons[0] = [(PE, PE.n)]
    qcons[1] = [(PE, PE.n)]
    x_users = [w for lst in x_users for w in lst]
    ve_a_end[0] = VE.n

    seq = [(pi, pn, et) for pi, pn in enumerate("kvqg") for et in range(ET)]
    g_first = next(i for i, (_, pn, _) in enumerate(seq) if pn == "g")
    nhoist = 2 if ET > 1 else 1
    ti = None
    for bi, (pi, pname, et) in enumerate(seq[nhoist:], start=nhoist):
        sl = bi % 2
        wl = dma("gpsimd", qbufs[sl][:, :].rearrange("p (c e) -> p c e", c=DC),
                 wqkvg[pname][et, :, :, :], GQ[sl], waits=list(qcons[sl]))
        for tc in range(TC):
            emit_qkvg_tc(pi, pname, et, sl, wl, tc)
            if bi < g_first:
                prep_tick()
                prep_tick()
        qcons[sl] = [(PE, PE.n)]
        if bi == g_first - 1:
            prep_drain()
            prep_done_waits = [(s, s.n) for s in SWF] + [(s, s.n) for s in SB]
        if pname == "v" and et == ET - 1:
            tot_evacs = []
            for et2 in range(ET):
                bk = banks[4 + et2 % 2]
                for tc in range(TC):
                    mm(oc_sb[:, :],
                       hid[:, kv_of + tc * D + et2 * EW:kv_of + tc * D + (et2 + 1) * EW],
                       bk, bk.t[0:1, 0:EW], start=(tc == 0), stop=(tc == TC - 1),
                       waits=[kv_et_done[et2]] if tc == 0 else [])
                ev = evac("vector", VE,
                          lambda e, o=pfx[0:1, et2 * EW:(et2 + 1) * EW],
                                 s=bk.t[0:1, 0:EW]: e.tensor_copy(o, s), bk,
                          waits=x_users if et2 == 0 else [])
                tot_evacs.append((VE, ev[VE]))
            ti = dma("sync", cc_in[:, :], pfx[0:1, :], CIN, waits=tot_evacs)
            prog.ev("gpsimd",
                    lambda e: e.collective_compute(
                        "AllGather", ALU.bypass, replica_groups=[list(range(NC))],
                        ins=[cc_in.ap().opt()], outs=[cc_out.ap().opt()]),
                    waits=[(CIN, ti[CIN])], incs=[(CC2, 1)])
    assert prep_back[0] == NSTEPS
    prog.ev("gpsimd",
            lambda e: e.collective_compute(
                "AllGather", ALU.bypass,
                replica_groups=[list(range(NC))],
                ins=[w2f_in.ap().opt()], outs=[w2f_g.ap().opt()]),
            waits=[(s, s.n) for s in SWF], incs=[(CC1, 1)])

    gl = dma("sync", x_st[0:NC, :], cc_out[:, :], GG,
             waits=[(CC2, 1)] + x_users)
    gc = prog.ev("vector",
                 lambda e: e.tensor_copy(gat[:, :], x_st[0:NC, :]),
                 waits=[(GG, gl[GG])], incs=[(VE, 1)])
    x_users = [(VE, gc[VE])]
    ms_evacs = []
    for et in range(ET):
        bk = banks[6 + et % 2]
        mm(rm_sb[:, :], gat[:, et * EW:(et + 1) * EW], bk, bk.t[0:1, 0:EW],
           waits=[(VE, gc[VE])])
        ev = evac("vector", VE,
                  lambda e, o=pfx[0:1, et * EW:(et + 1) * EW],
                         s=bk.t[0:1, 0:EW]: e.tensor_copy(o, s),
                  bk, waits=[(CIN, ti[CIN])])
        ms_evacs.append((VE, ev[VE]))
    pc_state = [prog.ev("scalar",
                        lambda e: e.activation(pfx_bf[:, :], pfx[0:1, :],
                                               AF.Identity),
                        waits=ms_evacs, incs=[(SE, 1)])]

    def emit_c_chunk(tc):
        pc = pc_state[0]
        for et in range(ET):
            bk = banks[4 + et % 2]
            mm(u_sb[:, :],
               hid[:, kv_of + tc * D + et * EW:kv_of + tc * D + (et + 1) * EW],
               bk, bk.t[:, 0:EW], start=True, stop=False,
               waits=[(SE, pc[SE])])
            mm(or_sb[:, :], pfx_bf[0:1, et * EW:(et + 1) * EW],
               bk, bk.t[:, 0:EW], start=False, stop=True)
            sl = et % 2
            e1 = evac("vector", VE,
                      lambda e, o=st[sl][:, 0:EW],
                             qv=hid[:, q_of + tc * D + et * EW:
                                    q_of + tc * D + (et + 1) * EW],
                             s=bk.t[:, 0:EW]: e.tensor_mul(o, qv, s), bk,
                      waits=[q_done[(tc, et)]] +
                            ([st_rd[sl]] if st_rd[sl] else []))
            e2 = prog.ev("vector",
                         lambda e, o=g_s[:, tc * D + et * EW:tc * D + (et + 1) * EW],
                                a=st[sl][:, 0:EW]: e.tensor_mul(o, o, a),
                         waits=[g_done[(tc, et)]], incs=[(VE, 1)])
            st_rd[sl] = (VE, e2[VE])
        if tc < TC - 1:
            adds = []
            for et in range(ET):
                bk = banks[6 + et % 2]
                mm(oc_sb[:, :],
                   hid[:, kv_of + tc * D + et * EW:kv_of + tc * D + (et + 1) * EW],
                   bk, bk.t[0:1, 0:EW], start=True, stop=True)
                ad = evac("vector", VE,
                          lambda e, o=pfx[0:1, et * EW:(et + 1) * EW],
                                 s=bk.t[0:1, 0:EW]: e.tensor_add(o, o, s), bk)
                adds.append((VE, ad[VE]))
            pc_state[0] = prog.ev("scalar",
                                  lambda e: e.activation(pfx_bf[:, :],
                                                         pfx[0:1, :], AF.Identity),
                                  waits=adds, incs=[(SE, 1)])

    Q = DC * 128
    w13bufs = [(wbuf[:, 0:Q], wbuf[:, Q:2 * Q]),
               (wbuf[:, 2 * Q:3 * Q], wbuf[:, 3 * Q:4 * Q]),
               (aux[:, 0:Q], aux[:, Q:2 * Q]),
               (aux[:, 2 * Q:3 * Q], aux[:, 3 * Q:4 * Q])]
    w2fbufs = [wbuf[:, 0:AQ * DSL], wbuf[:, AQ * DSL:2 * AQ * DSL]]
    w2f_g_r = w2f_g.ap().rearrange("(r b a) d -> r b a d", b=128, a=HC)
    out_prev = None
    wbuf_free = list(qcons[0])
    aux_free = list(qcons[1])
    hid_w = {}
    w13cons = [[], [], [], []]
    aux_fin = [[], []]
    s_all = None

    def emit_w13(hf, al):
        sl = al % 4
        if w13cons[sl]:
            ws = list(w13cons[sl])
        elif sl < 2:
            ws = list(wbuf_free)
        else:
            ws = list(aux_free) + list(aux_fin[hf])
        l1 = dma("gpsimd", w13bufs[sl][0][:, :].rearrange("p (c j) -> p c j", c=DC),
                 w1s[al, :, :, :], GL[sl], waits=ws)
        l3 = dma("gpsimd", w13bufs[sl][1][:, :].rearrange("p (c j) -> p c j", c=DC),
                 w3s[al, :, :, :], GL[sl])
        bg = banks[0 + 2 * (al % 2)]
        bl = banks[1 + 2 * (al % 2)]
        for dcc in range(DC):
            mm(w13bufs[sl][0][:, dcc * 128:(dcc + 1) * 128],
               xnT[:, dcc * T + hf * TH:dcc * T + hf * TH + TH],
               bg, bg.t[:, 0:TH], start=(dcc == 0), stop=(dcc == DC - 1),
               waits=[(GL[sl], l3[GL[sl]])] if dcc == 0 else [])
        for dcc in range(DC):
            mm(w13bufs[sl][1][:, dcc * 128:(dcc + 1) * 128],
               xnT[:, dcc * T + hf * TH:dcc * T + hf * TH + TH],
               bl, bl.t[:, 0:TH], start=(dcc == 0), stop=(dcc == DC - 1),
               waits=[(GL[sl], l3[GL[sl]])] if dcc == 0 else [])
        w13cons[sl] = [(PE, PE.n)]
        ssl = al % 2
        sv = evac("scalar", SE,
                  lambda e, o=st[ssl][:, 0:TH], s=bg.t[:, 0:TH]:
                      e.activation(o, s, AF.Silu), bg,
                  waits=[st_rd[ssl]] if st_rd[ssl] else [])
        hw = evac("vector", VE,
                  lambda e, o=hid[:, al * TH:(al + 1) * TH],
                         s=bl.t[:, 0:TH], a=st[ssl][:, 0:TH]:
                      e.tensor_mul(o, a, s), bl,
                  waits=[(SE, sv[SE])])
        st_rd[ssl] = (VE, hw[VE])
        hid_w[al] = (VE, hw[VE])

    for hf in range(2):
        if hf == 1:
            aux_fin[1] = [(VE, VE.n)]
        if hf == 0:
            for tc in range(TC):
                emit_c_chunk(tc)
                for al in range(4 * tc, 4 * tc + 4):
                    if al < HC:
                        emit_w13(0, al)
            for al in range(4 * TC, HC):
                emit_w13(0, al)
            s_all = (VE, VE.n)
        else:
            for al in range(HC):
                emit_w13(1, al)
        pe_after_w13 = PE.n
        w2fcons = [w13cons[0][0] if w13cons[0] else (PE, pe_after_w13),
                   w13cons[1][0] if w13cons[1] else (PE, pe_after_w13)]
        for dp in range(NC):
            obanks = [banks[4 + 2 * (dp % 2) + dcb] for dcb in range(DPC)]
            for aq0 in range(0, HC, AQ):
                sl = (aq0 // AQ) % 2
                ws = [(CC1, 1), w2fcons[sl]]
                wl = dma("sync",
                         w2fbufs[sl][:, :].rearrange("p (a d) -> p a d", a=AQ),
                         w2f_g_r[dp, :, aq0:aq0 + AQ, :], SWS[sl], waits=ws)
                for ai in range(AQ):
                    al = aq0 + ai
                    for dcb in range(DPC):
                        mm(w2fbufs[sl][:, ai * DSL + dcb * 128:
                                       ai * DSL + (dcb + 1) * 128],
                           hid[:, al * TH:(al + 1) * TH],
                           obanks[dcb], obanks[dcb].t[:, 0:TH],
                           start=(al == 0), stop=(al == HC - 1),
                           waits=([(SWS[sl], wl[SWS[sl]]), hid_w[al]] if ai == 0
                                  else [hid_w[al]]))
                w2fcons[sl] = (PE, PE.n)
            for dcb in range(DPC):
                bk = obanks[dcb]
                ssl = dcb % 2
                sv = evac("scalar", SE,
                          lambda e, o=st[ssl][:, 0:TH], s=bk.t[:, 0:TH]:
                              e.activation(o, s, AF.Silu), bk,
                          waits=[st_rd[ssl]] if st_rd[ssl] else [])
                tb = banks[dcb % 2]
                views = transpose_grp(
                    [st[ssl][:, j * 128:(j + 1) * 128] for j in range(THC)],
                    tb, waits=[(SE, sv[SE])])
                st_rd[ssl] = (PE, PE.n)
                doff = (dp * DPC + dcb) * 128
                gdst = aux[:, :].rearrange("p (j d) -> p j d", d=D)[
                    :, 0:THC, doff:doff + 128]
                gsrc = tb.t[:, 0:THC * 64].bitcast(BF16).rearrange(
                    "p (j t) -> p j t", t=128)
                evac("vector", VE,
                     lambda e, o=gdst, s=gsrc: e.tensor_copy(o, s),
                     tb,
                     waits=aux_free if (hf == 0 and dp == 0 and dcb == 0)
                           else [])
            if dp == NC // 2 - 1:
                for j in range(THC):
                    tcc = hf * THC + j
                    prog.ev("vector",
                            lambda e, j=j, tcc=tcc: e.tensor_mul(
                                xn_st[:, j * (D // 2):(j + 1) * (D // 2)],
                                g_s[:, tcc * D:tcc * D + D // 2],
                                aux[:, j * D:j * D + D // 2]),
                            waits=[s_all], incs=[(VE, 1)])
        g2_evd = (VE, VE.n)
        wbuf_free = [(PE, PE.n)]
        w13cons = [[], [], [], []]
        HD = D // 2
        if hf == 1:
            fx = [x_st, wbuf[:, 0:4096].bitcast(F32), wbuf[:, 4096:8192].bitcast(F32)]
            fxw = [list(x_users) + list(wbuf_free), list(wbuf_free), list(wbuf_free)]
            fsx = [SXs[0], SXs[1], SXs[2]]
        else:
            fx = [x_st]
            fxw = [list(x_users)]
            fsx = [SXs[0]]
        nfx = len(fx)
        for j in range(THC):
            tc = hf * THC + j
            bi_ = j % nfx
            xb = fx[bi_]
            SX = fsx[bi_]
            xl = dma("sync", xb[:, :], x_tok[tc * 128:(tc + 1) * 128, :], SX,
                     waits=fxw[bi_])
            o1 = prog.ev("vector",
                         lambda e, j=j, xx=xb: e.scalar_tensor_tensor(
                             o_st[:, 0:HD], xn_st[:, j * HD:(j + 1) * HD], 1.0,
                             xx[:, 0:HD], op0=ALU.mult, op1=ALU.add),
                         waits=[(SX, xl[SX]), (SE, pc_state[0][SE])] +
                               ([out_prev] if out_prev else []),
                         incs=[(VE, 1)])
            prog.ev("vector",
                    lambda e, j=j, tcc=tc: e.tensor_mul(
                        xn_st[:, j * HD:(j + 1) * HD],
                        g_s[:, tcc * D + HD:(tcc + 1) * D],
                        aux[:, j * D + HD:(j + 1) * D]),
                    waits=[g2_evd], incs=[(VE, 1)])
            ou = prog.ev("vector",
                         lambda e, j=j, xx=xb: e.scalar_tensor_tensor(
                             o_st[:, HD:D], xn_st[:, j * HD:(j + 1) * HD], 1.0,
                             xx[:, HD:D], op0=ALU.mult, op1=ALU.add),
                         incs=[(VE, 1)])
            fxw[bi_] = [(VE, ou[VE])]
            x_users = [(VE, ou[VE])]
            so = dma("sync", out_tok[tc * 128:(tc + 1) * 128, :], o_st[:, :], FIN,
                     waits=[(VE, ou[VE])])
            out_prev = (FIN, so[FIN])

    prog.ev("gpsimd", lambda e: e.nop(), waits=[(FIN, FIN.n)])

    prog.realize(nc, ctx)
    ctx.close()
    return nc


def _hadamard(n):
    h = np.array([[1.0]])
    while h.shape[0] < n:
        h = np.block([[h, h], [h, -h]])
    return h


def _tile_qkvg(W, cfg):
    WT = np.asarray(W, np.float32).T
    A = WT.reshape(cfg.DC, 128, cfg.ET, cfg.EW)
    return np.ascontiguousarray(A.transpose(2, 1, 0, 3))


def _tile_w13(W, cfg):
    WT = np.asarray(W, np.float32).T
    A = WT.reshape(cfg.DC, 128, cfg.HC, 128)
    return np.ascontiguousarray(A.transpose(2, 1, 0, 3))


def make_inputs(cfg, x, Wq, bq, Wk, bk, Wv, bv, Wg, bg, W1, W2, W3):
    f32 = np.float32
    x2 = np.ascontiguousarray(np.asarray(x, f32).reshape(cfg.S, cfg.D))
    W2T = np.ascontiguousarray(np.asarray(W2, f32).T)
    tiles = {p: _tile_qkvg(w, cfg) for p, w in
             (("k", Wk), ("v", Wv), ("q", Wq), ("g", Wg))}
    w1t = _tile_w13(W1, cfg)
    w3t = _tile_w13(W3, cfg)
    bias4 = np.stack([np.asarray(b, f32) for b in (bk, bv, bq, bg)])
    u = np.triu(np.ones((128, 128))).astype(BF)
    ident = np.eye(128).astype(BF)
    hamat_a = (_hadamard(cfg.HC) / 2.0 ** cfg.a_exp).astype(BF)
    hamat_b = (_hadamard(128) / 2.0 ** cfg.b_exp).astype(BF)
    onesr = np.ones((1, 128)).astype(BF)
    onesc = np.ones((128, 1)).astype(BF)
    pmask = np.zeros((4, 4 * 128))
    for p in range(4):
        pmask[p, p * 128:(p + 1) * 128] = 1.0
    pmask = pmask.astype(BF)
    in_maps = []
    for c in range(cfg.NC):
        sl = np.ascontiguousarray(
            W2T[:, c * cfg.DSL:(c + 1) * cfg.DSL].reshape(cfg.HC, 128 * cfg.DSL))
        rm = np.zeros((cfg.NC, 1))
        rm[:c, 0] = 1.0
        in_maps.append(dict(
            x_tok=np.ascontiguousarray(x2[c * cfg.T:(c + 1) * cfg.T, :]),
            wks=tiles["k"], wvs=tiles["v"], wqs=tiles["q"], wgs=tiles["g"],
            bias4=bias4, w1s=w1t, w3s=w3t, w2ts=sl,
            u128=u, identb=ident, ha=hamat_a, hb=hamat_b,
            onesr=onesr, onesc=onesc, pmask=pmask, rmask=rm.astype(BF),
        ))
    return in_maps


def kernel(L0_attn_x, L0_attn_Wq, L0_attn_bq, L0_attn_Wk, L0_attn_bk,
           L0_attn_Wv, L0_attn_bv, L0_attn_Wg, L0_attn_bg,
           L0_ffn_W1, L0_ffn_W2, L0_ffn_W3):
    cfg = Cfg(S=8192, D=2048, H=8192, NC=8)
    in_maps = make_inputs(cfg, L0_attn_x, L0_attn_Wq, L0_attn_bq, L0_attn_Wk,
                          L0_attn_bk, L0_attn_Wv, L0_attn_bv, L0_attn_Wg,
                          L0_attn_bg, L0_ffn_W1, L0_ffn_W2, L0_ffn_W3)
    nc = build(cfg)
    res = run_bass_kernel_spmd(nc, in_maps, core_ids=list(range(cfg.NC)))
    out = np.concatenate([r["out_tok"] for r in res.results], axis=0)
    return out.reshape(1, cfg.S, cfg.D).astype(np.float32)


# version 19
# speedup vs baseline: 1.0217x; 1.0217x over previous
import math
from contextlib import ExitStack

import numpy as np
import ml_dtypes

import concourse.bass as bass
import concourse.mybir as mybir
from concourse.bass_utils import run_bass_kernel_spmd

BF16 = mybir.dt.bfloat16
F32 = mybir.dt.float32
AF = mybir.ActivationFunctionType
ALU = mybir.AluOpType
BF = ml_dtypes.bfloat16


class Cfg:
    def __init__(self, S=8192, D=2048, H=8192, NC=8):
        assert H == 4 * D
        self.S, self.D, self.H, self.NC = S, D, H, NC
        self.T = S // NC
        self.TC = self.T // 128
        self.TH = self.T // 2
        self.THC = self.TC // 2
        self.DC = D // 128
        self.HC = H // 128
        self.ET = max(1, D // 512)
        self.EW = min(512, D)
        self.DSL = D // NC
        assert self.DSL % 128 == 0
        self.DPC = self.DSL // 128
        self.AQ = 16 if self.HC % 16 == 0 else self.HC
        self.a_exp = int(round(math.log2(self.HC) / 2))
        self.b_exp = 4
        self.fix = (2.0 ** (self.a_exp + self.b_exp)) / math.sqrt(H)
        pw = 1024
        while 8 * pw + 8 * self.DSL > self.TC * D:
            pw //= 2
        assert pw >= 128
        self.PW = pw
        self.NPC = 128 * self.DSL // pw


class Ctr:
    __slots__ = ("name", "n", "h")

    def __init__(self, name):
        self.name, self.n, self.h = name, 0, None


class Bank:
    __slots__ = ("idx", "t", "writer", "readers")

    def __init__(self, idx, t):
        self.idx, self.t = idx, t
        self.writer = None
        self.readers = {}


class Prog:

    def __init__(self):
        self.events = []
        self.ctrs = []
        self.chain = {}

    def ctr(self, name):
        c = Ctr(name)
        self.ctrs.append(c)
        return c

    def ev(self, eng, fn, waits=(), incs=(), chain=True):
        ics_in = list(incs)
        ch = self.chain.get(eng) if chain else None
        if ch is not None and not any(c is ch for (c, _) in ics_in):
            ics_in.append((ch, 1))
        ws = [(c, v) for (c, v) in waits if v > 0]
        out = {}
        ics = []
        for (c, n) in ics_in:
            c.n += n
            ics.append((c, n))
            out[c] = c.n
        self.events.append((eng, ws, fn, ics, chain))
        return out

    def check(self):
        order = ["sync", "gpsimd", "tensor", "vector", "scalar"]
        qs = {e: [ev for ev in self.events if ev[0] == e] for e in order}
        pos = {e: 0 for e in order}
        cnt = {c: 0 for c in self.ctrs}
        progressed = True
        while progressed:
            progressed = False
            for e in order:
                while pos[e] < len(qs[e]):
                    (_, ws, fn, ics, _ch) = qs[e][pos[e]]
                    if any(cnt[c] < v for (c, v) in ws):
                        break
                    for (c, n) in ics:
                        cnt[c] += n
                    pos[e] += 1
                    progressed = True
        stuck = {e: (len(qs[e]) - pos[e]) for e in order if pos[e] < len(qs[e])}
        if stuck:
            lines = []
            for e in stuck:
                (_, ws, fn, ics, _ch) = qs[e][pos[e]]
                bad = [(c.name, v, cnt[c]) for (c, v) in ws if cnt[c] < v]
                lines.append(f"{e}@{pos[e]}: needs {bad}")
            raise RuntimeError("deadlock in event graph: " + "; ".join(lines))

    def realize(self, nc, ctx):
        self.check()
        for c in self.ctrs:
            c.h = ctx.enter_context(nc.semaphore(c.name))
        block = ctx.enter_context(nc.Block())
        order = ["sync", "gpsimd", "tensor", "vector", "scalar"]
        evs_by = {e: [] for e in order}
        for (eng, ws, fn, ics, ch) in self.events:
            evs_by[eng].append((ws, fn, ics, ch))

        def make(evs, ch):
            def body(eng):
                seen = {}
                k = 0
                for (ws, fn, ics, chained) in evs:
                    if ch is not None and chained and k > 0:
                        eng.wait_ge(ch.h, k)
                        seen[ch] = k
                    for (c, v) in ws:
                        if seen.get(c, 0) < v:
                            eng.wait_ge(c.h, v)
                            seen[c] = v
                    inst = fn(eng)
                    for (c, n) in ics:
                        inst = inst.then_inc(c.h, n)
                    if ch is not None and chained:
                        k += 1
            return body

        for e in order:
            getattr(block, e)(make(evs_by[e], self.chain.get(e)))


def build(cfg: Cfg):
    nc = bass.Bass("TRN2", target_bir_lowering=False)
    S, D, H, NC = cfg.S, cfg.D, cfg.H, cfg.NC
    T, TC, TH, THC = cfg.T, cfg.TC, cfg.TH, cfg.THC
    DC, HC, ET, EW = cfg.DC, cfg.HC, cfg.ET, cfg.EW
    DSL, DPC, AQ = cfg.DSL, cfg.DPC, cfg.AQ
    PW, NPC = cfg.PW, cfg.NPC

    ein = lambda n, s, dt: nc.dram_tensor(n, s, dt, kind="ExternalInput")
    x_tok = ein("x_tok", [T, D], F32)
    wqkvg = {p: ein(f"w{p}s", [ET, 128, DC, EW], F32) for p in "kvqg"}
    bias4 = ein("bias4", [4, D], F32)
    w1s = ein("w1s", [HC, 128, DC, 128], F32)
    w3s = ein("w3s", [HC, 128, DC, 128], F32)
    w2ts = ein("w2ts", [HC, 128 * DSL], F32)
    u128 = ein("u128", [128, 128], BF16)
    identb = ein("identb", [128, 128], BF16)
    ha = ein("ha", [HC, HC], BF16)
    hb = ein("hb", [128, 128], BF16)
    onesr = ein("onesr", [1, 128], BF16)
    onesc = ein("onesc", [128, 1], BF16)
    pmask = ein("pmask", [4, 4 * 128], BF16)
    rmask = ein("rmask", [NC, 1], BF16)
    out_tok = nc.dram_tensor("out_tok", [T, D], F32, kind="ExternalOutput")

    w13b = nc.dram_tensor("w13b", [2, HC, 128, DC * 128], BF16)
    w2b = nc.dram_tensor("w2b", [HC, 128 * DSL], BF16)
    w2f_in = nc.dram_tensor("w2f_in", [H, DSL], BF16)
    w2f_g = nc.dram_tensor("w2f_g", [NC * H, DSL], BF16, addr_space="Shared")
    cc_in = nc.dram_tensor("cc_in", [1, D], F32)
    cc_out = nc.dram_tensor("cc_out", [NC, D], F32)

    prog = Prog()
    PE = prog.ctr("pe")
    VE = prog.ctr("ve")
    SE = prog.ctr("se")
    prog.chain = {"vector": VE, "scalar": SE}
    CC1 = prog.ctr("cc1")
    CC2 = prog.ctr("cc2")
    FIN = prog.ctr("fin")
    CST = prog.ctr("cst")
    GB4 = prog.ctr("gb4")
    GP = [prog.ctr(f"gp{i}") for i in range(4)]
    SB = [prog.ctr(f"sb{i}") for i in range(4)]
    SA = [prog.ctr(f"sa{i}") for i in range(4)]
    SWF = [prog.ctr(f"swf{i}") for i in range(4)]
    GQ = [prog.ctr(f"gq{i}") for i in range(4)]
    GL = [prog.ctr(f"gl{i}") for i in range(4)]
    GW13 = prog.ctr("gw13")
    SWS = [prog.ctr("sws0"), prog.ctr("sws1")]
    GG = prog.ctr("gg")
    CIN = prog.ctr("cin")
    SXs = [prog.ctr(f"sx{i}") for i in range(5)]

    ctx = ExitStack()
    sb = lambda n, s, dt: ctx.enter_context(nc.sbuf_tensor(n, s, dt))

    xnT = sb("xnT", [128, DC * T], BF16)
    hid = sb("hid", [128, 2 * TC * D], BF16)
    kv_of = 0
    q_of = TC * D
    g_s = sb("g_s", [128, TC * D], BF16)
    wbuf = sb("wbuf", [128, 8192], BF16)
    aux = sb("aux", [128, 8192], BF16)
    b4 = sb("b4", [4, D], BF16)
    pm = sb("pm", [4, 4 * 128], BF16)
    gat = sb("gat", [NC, D], BF16)
    st = [sb("st1", [128, 512], BF16), sb("st2", [128, 512], BF16)]
    xn_st = sb("xn_st", [128, 2 * D], BF16)
    pfx_bf = sb("pfx_bf", [1, D], BF16)
    u_sb = sb("u_sb", [128, 128], BF16)
    id_sb = sb("id_sb", [128, 128], BF16)
    ha_sb = sb("ha_sb", [HC, HC], BF16)
    hb_sb = sb("hb_sb", [128, 128], BF16)
    or_sb = sb("or_sb", [1, 128], BF16)
    oc_sb = sb("oc_sb", [128, 1], BF16)
    rm_sb = sb("rm_sb", [NC, 1], BF16)
    x_st = sb("x_st", [128, D], F32)
    o_st = sb("o_st", [128, D], F32)
    pfx = o_st[0:1, :]
    rows = sb("rows", [128, 16], F32)

    banks = [Bank(i, ctx.enter_context(nc.psum_tensor(f"pb{i}", [128, 512], F32)))
             for i in range(8)]
    st_rd = [None, None]

    def dma(queue, dst, src, c, waits=()):
        return prog.ev(queue, lambda e, d=dst, s=src: e.dma_start(out=d, in_=s),
                       waits=waits, incs=[(c, 16)], chain=False)

    def mm(lhsT, rhs, bank, view, start=True, stop=True, waits=()):
        ws = list(waits)
        if start:
            for c, v in bank.readers.items():
                ws.append((c, v))
            bank.readers = {}
        r = prog.ev("tensor",
                    lambda e, o=view, l=lhsT, rr=rhs, s1=start, s2=stop:
                        e.matmul(o, l, rr, start=s1, stop=s2),
                    waits=ws, incs=[(PE, 1)])
        if stop:
            bank.writer = (PE, r[PE])
        return r

    def evac(eng_name, c, fn, bank, waits=()):
        ws = list(waits)
        if bank.writer is not None:
            ws.append(bank.writer)
        r = prog.ev(eng_name, fn, waits=ws, incs=[(c, 1)])
        bank.readers[c] = r[c]
        return r

    def transpose_grp(src_aps, bank, waits=()):
        views, r = [], None
        for j, sap in enumerate(src_aps):
            view = bank.t[:, j * 64:(j + 1) * 64].bitcast(BF16)
            ws = []
            if j == 0:
                ws = list(waits)
                for c, v in bank.readers.items():
                    ws.append((c, v))
                bank.readers = {}
            r = prog.ev("tensor",
                        lambda e, o=view, s=sap: e.transpose(o, s, id_sb[:, :]),
                        waits=ws, incs=[(PE, 1)])
            views.append(view)
        bank.writer = (PE, r[PE])
        return views

    dma("sync", u_sb[:, :], u128[:, :], CST)
    dma("sync", id_sb[:, :], identb[:, :], CST)
    dma("sync", ha_sb[:, :], ha[:, :], CST)
    dma("sync", hb_sb[:, :], hb[:, :], CST)
    dma("sync", or_sb[:, :], onesr[:, :], CST)
    dma("sync", oc_sb[:, :], onesc[:, :], CST)
    dma("sync", rm_sb[:, :], rmask[:, :], CST)
    dma("sync", pm[:, :], pmask[:, :], CST)
    consts_ld = (CST, CST.n)
    dma("gpsimd", b4[:, :], bias4[:, :], GB4)
    bias_gw = (GB4, GB4.n)

    LEAD = 3
    NPS = NAS = LEAD + 1
    p_in = [g_s[:, i * PW:(i + 1) * PW] for i in range(NPS)]
    p_out = [g_s[:, (NPS + i) * PW:(NPS + i + 1) * PW] for i in range(NPS)]
    a_base = 2 * NPS * PW
    a_in = [g_s[:, a_base + i * DSL:a_base + (i + 1) * DSL] for i in range(NAS)]
    a_out = [g_s[:, a_base + (NAS + i) * DSL:a_base + (NAS + i + 1) * DSL]
             for i in range(NAS)]
    prep_state = dict(piece_cons=[None] * NPS, w2b_w=[None] * NPS,
                      a_cons=[None] * NAS, w2f_w=[None] * NAS, all_w2b=None,
                      ldv={})
    w2b_r = w2b.ap().rearrange("a (b d) -> a b d", b=128)
    w2f_in_r = w2f_in.ap().rearrange("(b a) d -> b a d", a=HC)
    BN = min(512, PW)
    NSTEPS = NPC + HC
    prep_front = [0]
    prep_back = [0]

    def prep_load(i):
        stt = prep_state
        if i < NPC:
            sl = i % NPS
            ws = [stt["piece_cons"][sl]] if stt["piece_cons"][sl] else []
            ld = dma("gpsimd", p_in[sl][0:HC, :], w2ts[:, i * PW:(i + 1) * PW],
                     GP[sl], waits=ws)
            stt["ldv"][i] = (GP[sl], ld[GP[sl]])
        else:
            al = i - NPC
            sl = al % NAS
            ws = list(stt["all_w2b"])
            if stt["a_cons"][sl]:
                ws.append(stt["a_cons"][sl])
            ld = dma("sync", a_in[sl][:, :], w2b_r[al, :, :], SA[sl], waits=ws)
            stt["ldv"][i] = (SA[sl], ld[SA[sl]])

    def prep_compute(i):
        stt = prep_state
        if i < NPC:
            sl = i % NPS
            ev = r = None
            for j in range(PW // BN):
                bk = banks[6 + (j % 2)]
                r = mm(ha_sb[:, :], p_in[sl][0:HC, j * BN:(j + 1) * BN],
                       bk, bk.t[0:HC, 0:BN],
                       waits=[stt["ldv"][i], consts_ld])
                ev = evac("scalar", SE,
                          lambda e, o=p_out[sl][0:HC, j * BN:(j + 1) * BN],
                                 s=bk.t[0:HC, 0:BN]: e.activation(o, s, AF.Identity),
                          bk, waits=[stt["w2b_w"][sl]]
                          if (j == 0 and stt["w2b_w"][sl]) else [])
            stt["piece_cons"][sl] = (PE, r[PE])
            wd = dma("scalar", w2b[:, i * PW:(i + 1) * PW], p_out[sl][0:HC, :],
                     SB[sl], waits=[(SE, ev[SE])])
            stt["w2b_w"][sl] = (SB[sl], wd[SB[sl]])
            if i == NPC - 1:
                stt["all_w2b"] = [(s, s.n) for s in SB]
        else:
            al = i - NPC
            sl = al % NAS
            bk = banks[6 + sl % 2]
            r = mm(hb_sb[:, :], a_in[sl][:, :], bk, bk.t[:, 0:DSL],
                   waits=[stt["ldv"][i], consts_ld])
            stt["a_cons"][sl] = (PE, r[PE])
            ev = evac("scalar", SE,
                      lambda e, o=a_out[sl][:, :], s=bk.t[:, 0:DSL]:
                          e.activation(o, s, AF.Identity, scale=float(cfg.fix)),
                      bk, waits=[stt["w2f_w"][sl]] if stt["w2f_w"][sl] else [])
            wd = dma("scalar", w2f_in_r[:, al, :], a_out[sl][:, :], SWF[sl],
                     waits=[(SE, ev[SE])])
            stt["w2f_w"][sl] = (SWF[sl], wd[SWF[sl]])

    def prep_tick():
        f, b = prep_front[0], prep_back[0]
        if f < NSTEPS and not (f >= NPC and b < NPC):
            prep_load(f)
            prep_front[0] = f = f + 1
        if b < f and (f - b > LEAD or f == NSTEPS or (f == NPC and b < NPC)):
            prep_compute(b)
            prep_back[0] = b + 1

    def prep_drain():
        while prep_back[0] < NSTEPS:
            prep_tick()

    qbufs = [wbuf[:, 0:DC * EW], aux[:, 0:DC * EW]]
    ve_a_end = [0]
    qcons = [[], []]
    kv_et_done = [None] * ET
    q_done, g_done = {}, {}
    xn_done = [None] * TC
    prep_done_waits = None

    def load_qkvg(pname, et, sl, waits=()):
        h = DC // 2
        la = dma("gpsimd",
                 qbufs[sl][:, 0:h * EW].rearrange("p (c e) -> p c e", c=h),
                 wqkvg[pname][et, :, 0:h, :], GQ[2 * sl], waits=list(waits))
        lb = dma("gpsimd",
                 qbufs[sl][:, h * EW:].rearrange("p (c e) -> p c e", c=h),
                 wqkvg[pname][et, :, h:, :], GQ[2 * sl + 1])
        return ((GQ[2 * sl], la[GQ[2 * sl]]), (GQ[2 * sl + 1], lb[GQ[2 * sl + 1]]))

    def emit_qkvg_tc(pi, pname, et, sl, wl, tc, nbank=2):
        bk = banks[2 + tc % nbank]
        for dcc in range(DC):
            ws = []
            if dcc == 0:
                ws = [wl[0], xn_done[tc]]
            elif dcc == DC // 2:
                ws = [wl[1]]
            mm(xnT[:, dcc * T + tc * 128:dcc * T + (tc + 1) * 128],
               qbufs[sl][:, dcc * EW:(dcc + 1) * EW],
               bk, bk.t[:, 0:EW], start=(dcc == 0), stop=False,
               waits=ws)
        mm(pm[:, pi * 128:(pi + 1) * 128], b4[:, et * EW:(et + 1) * EW],
           bk, bk.t[:, 0:EW], start=False, stop=True, waits=[bias_gw])
        if pname == "k":
            evac("scalar", SE,
                 lambda e, o=hid[:, kv_of + tc * D + et * EW:
                                 kv_of + tc * D + (et + 1) * EW],
                        s=bk.t[:, 0:EW]: e.activation(o, s, AF.Identity), bk)
        elif pname == "v":
            kvv = hid[:, kv_of + tc * D + et * EW:kv_of + tc * D + (et + 1) * EW]
            ev = evac("vector", VE,
                      lambda e, o=kvv, s=bk.t[:, 0:EW]: e.tensor_mul(o, o, s),
                      bk, waits=[(SE, SE.n)])
            if tc == TC - 1:
                kv_et_done[et] = (VE, ev[VE])
        elif pname == "q":
            ev = evac("scalar", SE,
                      lambda e, o=hid[:, q_of + tc * D + et * EW:
                                      q_of + tc * D + (et + 1) * EW],
                             s=bk.t[:, 0:EW]: e.activation(o, s, AF.Identity),
                      bk, waits=[(VE, max(s2_last, ve_a_end[0]))])
            q_done[(tc, et)] = (SE, ev[SE])
        else:
            ev = evac("scalar", SE,
                      lambda e, o=g_s[:, tc * D + et * EW:tc * D + (et + 1) * EW],
                             s=bk.t[:, 0:EW]: e.activation(o, s, AF.Sigmoid),
                      bk, waits=prep_done_waits)
            g_done[(tc, et)] = (SE, ev[SE])

    for _ in range(4):
        prep_tick()
    wl0 = load_qkvg("k", 0, 0)
    wl1 = load_qkvg("k", min(1, ET - 1), 1)
    r_eps = rows[:, 15:16]
    ev0 = prog.ev("vector", lambda e: e.memset(r_eps, 1e-5), incs=[(VE, 1)])
    eps_set = ev0[VE]
    se_v2_prev = [0, 0]
    nq = max(0, min(3, (TC - 2) // 2))
    xbufs = [x_st, o_st] + [
        hid[:, q_of + 2 * j * D:q_of + (2 * j + 2) * D].bitcast(F32)
        for j in range(nq)]
    NXB = len(xbufs)
    x_users = [[consts_ld] for _ in range(NXB)]
    s2scr = hid[:, q_of + (TC - 1) * D:q_of + TC * D]
    prev_tp = [0, 0]
    row_rd = [0, 0]
    s2_last = 0
    for tc in range(TC):
        par = tc % 2
        xbi = tc % NXB
        r_s1, r_s2 = rows[:, par * 7:par * 7 + 1], rows[:, par * 7 + 1:par * 7 + 2]
        r_m = rows[:, par * 7 + 2:par * 7 + 3]
        r_v = rows[:, par * 7 + 3:par * 7 + 4]
        r_v2 = rows[:, par * 7 + 4:par * 7 + 5]
        r_std = rows[:, par * 7 + 5:par * 7 + 6]
        r_rst = rows[:, par * 7 + 6:par * 7 + 7]
        xn_v = xn_st[:, par * D:(par + 1) * D]
        xb = xbufs[xbi]
        SX = SXs[xbi]
        if tc >= 2:
            emit_qkvg_tc(0, "k", 0, 0, wl0, tc - 2, nbank=4)
            if ET > 1:
                emit_qkvg_tc(0, "k", 1, 1, wl1, tc - 2, nbank=4)
        xl = dma("sync", xb[:, :], x_tok[tc * 128:(tc + 1) * 128, :], SX,
                 waits=x_users[xbi])
        sq = prog.ev("scalar",
                     lambda e, o=xn_v, xx=xb, r=r_s1: e.activation(
                         o, xx[:, :], AF.Identity, accum_out=r),
                     waits=[(SX, xl[SX]), (PE, prev_tp[par]), (VE, row_rd[par])],
                     incs=[(SE, 1)])
        s2 = prog.ev("vector",
                     lambda e, xx=xb, r=r_s2: e.scalar_tensor_tensor(
                         s2scr[:, :], xx[:, :], 1.0, xx[:, :],
                         op0=ALU.mult, op1=ALU.mult, accum_out=r),
                     waits=[(SX, xl[SX]), (SE, se_v2_prev[par])], incs=[(VE, 1)])
        s2_last = s2[VE]
        prog.ev("scalar",
                lambda e, a=r_m, b=r_s1: e.activation(a, b, AF.Identity,
                                                      scale=1.0 / D),
                waits=[(VE, row_rd[par])], incs=[(SE, 1)])
        prog.ev("scalar", lambda e, a=r_v, b=r_m: e.activation(a, b, AF.Square))
        prog.ev("scalar",
                lambda e, a=r_v: e.activation(a, a, AF.Identity, scale=-1.0,
                                              bias=r_eps),
                waits=[(VE, eps_set)])
        v2ev = prog.ev("scalar",
                       lambda e, a=r_v2, b=r_s2, c=r_v: e.activation(
                           a, b, AF.Identity, scale=1.0 / D, bias=c),
                       waits=[(VE, s2[VE])], incs=[(SE, 1)])
        se_v2_prev[par] = v2ev[SE]
        prog.ev("scalar", lambda e, a=r_std, b=r_v2: e.activation(a, b, AF.Ln))
        rc = prog.ev("scalar",
                     lambda e, a=r_rst, b=r_std: e.activation(a, b, AF.Exp,
                                                              scale=-0.5),
                     incs=[(SE, 1)])
        ap = prog.ev("vector",
                     lambda e, o=xn_v, xx=xb, a=r_m, b=r_rst: e.tensor_scalar(
                         o, xx[:, :], a, b,
                         op0=ALU.subtract, op1=ALU.mult),
                     waits=[(SE, rc[SE])], incs=[(VE, 1)])
        row_rd[par] = ap[VE]
        x_users[xbi] = [(VE, ap[VE]), (VE, s2[VE]), (SE, sq[SE])]
        ev = None
        xnT_r = xnT[:, :].rearrange("p (c t) -> p c t", t=T)
        for grp in range(DC // 4):
            bk = banks[grp % 2]
            views = transpose_grp(
                [xn_v[:, (grp * 4 + j) * 128:(grp * 4 + j + 1) * 128]
                 for j in range(4)],
                bk, waits=[(VE, ap[VE]), consts_ld])
            dst = xnT_r[:, grp * 4:(grp + 1) * 4, tc * 128:(tc + 1) * 128]
            srcv = bk.t[:, 0:256].bitcast(BF16).rearrange(
                "p (c t) -> p c t", t=128)
            ev = evac("vector", VE,
                      lambda e, o=dst, s=srcv: e.tensor_copy(o, s), bk)
        prev_tp[par] = PE.n
        xn_done[tc] = (VE, ev[VE])
        prep_tick()
        prep_tick()
    for tcl in (TC - 2, TC - 1):
        emit_qkvg_tc(0, "k", 0, 0, wl0, tcl, nbank=4)
        if ET > 1:
            emit_qkvg_tc(0, "k", 1, 1, wl1, tcl, nbank=4)
    qcons[0] = [(PE, PE.n)]
    qcons[1] = [(PE, PE.n)]
    x_users = [w for lst in x_users for w in lst]
    ve_a_end[0] = VE.n

    seq = [(pi, pn, et) for pi, pn in enumerate("kvqg") for et in range(ET)]
    g_first = next(i for i, (_, pn, _) in enumerate(seq) if pn == "g")
    nhoist = 2 if ET > 1 else 1
    ti = None
    for bi, (pi, pname, et) in enumerate(seq[nhoist:], start=nhoist):
        sl = bi % 2
        wl = load_qkvg(pname, et, sl, waits=qcons[sl])
        for tc in range(TC):
            emit_qkvg_tc(pi, pname, et, sl, wl, tc)
            if bi < g_first:
                prep_tick()
                prep_tick()
        qcons[sl] = [(PE, PE.n)]
        if bi == g_first - 1:
            prep_drain()
            prep_done_waits = [(s, s.n) for s in SWF] + [(s, s.n) for s in SB]
        if pname == "v" and et == ET - 1:
            tot_evacs = []
            for et2 in range(ET):
                bk = banks[4 + et2 % 2]
                for tc in range(TC):
                    mm(oc_sb[:, :],
                       hid[:, kv_of + tc * D + et2 * EW:kv_of + tc * D + (et2 + 1) * EW],
                       bk, bk.t[0:1, 0:EW], start=(tc == 0), stop=(tc == TC - 1),
                       waits=[kv_et_done[et2]] if tc == 0 else [])
                ev = evac("vector", VE,
                          lambda e, o=pfx[0:1, et2 * EW:(et2 + 1) * EW],
                                 s=bk.t[0:1, 0:EW]: e.tensor_copy(o, s), bk,
                          waits=x_users if et2 == 0 else [])
                tot_evacs.append((VE, ev[VE]))
            ti = dma("sync", cc_in[:, :], pfx[0:1, :], CIN, waits=tot_evacs)
            prog.ev("gpsimd",
                    lambda e: e.collective_compute(
                        "AllGather", ALU.bypass, replica_groups=[list(range(NC))],
                        ins=[cc_in.ap().opt()], outs=[cc_out.ap().opt()]),
                    waits=[(CIN, ti[CIN])], incs=[(CC2, 1)])
    assert prep_back[0] == NSTEPS
    prog.ev("gpsimd",
            lambda e: e.collective_compute(
                "AllGather", ALU.bypass,
                replica_groups=[list(range(NC))],
                ins=[w2f_in.ap().opt()], outs=[w2f_g.ap().opt()]),
            waits=[(s, s.n) for s in SWF], incs=[(CC1, 1)])

    gl = dma("sync", x_st[0:NC, :], cc_out[:, :], GG,
             waits=[(CC2, 1)] + x_users)
    gc = prog.ev("vector",
                 lambda e: e.tensor_copy(gat[:, :], x_st[0:NC, :]),
                 waits=[(GG, gl[GG])], incs=[(VE, 1)])
    x_users = [(VE, gc[VE])]
    ms_evacs = []
    for et in range(ET):
        bk = banks[6 + et % 2]
        mm(rm_sb[:, :], gat[:, et * EW:(et + 1) * EW], bk, bk.t[0:1, 0:EW],
           waits=[(VE, gc[VE])])
        ev = evac("vector", VE,
                  lambda e, o=pfx[0:1, et * EW:(et + 1) * EW],
                         s=bk.t[0:1, 0:EW]: e.tensor_copy(o, s),
                  bk, waits=[(CIN, ti[CIN])])
        ms_evacs.append((VE, ev[VE]))
    pc_state = [prog.ev("scalar",
                        lambda e: e.activation(pfx_bf[:, :], pfx[0:1, :],
                                               AF.Identity),
                        waits=ms_evacs, incs=[(SE, 1)])]

    def emit_c_chunk(tc):
        pc = pc_state[0]
        for et in range(ET):
            bk = banks[4 + et % 2]
            mm(u_sb[:, :],
               hid[:, kv_of + tc * D + et * EW:kv_of + tc * D + (et + 1) * EW],
               bk, bk.t[:, 0:EW], start=True, stop=False,
               waits=[(SE, pc[SE])])
            mm(or_sb[:, :], pfx_bf[0:1, et * EW:(et + 1) * EW],
               bk, bk.t[:, 0:EW], start=False, stop=True)
            sl = et % 2
            e1 = evac("vector", VE,
                      lambda e, o=st[sl][:, 0:EW],
                             qv=hid[:, q_of + tc * D + et * EW:
                                    q_of + tc * D + (et + 1) * EW],
                             s=bk.t[:, 0:EW]: e.tensor_mul(o, qv, s), bk,
                      waits=[q_done[(tc, et)]] +
                            ([st_rd[sl]] if st_rd[sl] else []))
            e2 = prog.ev("vector",
                         lambda e, o=g_s[:, tc * D + et * EW:tc * D + (et + 1) * EW],
                                a=st[sl][:, 0:EW]: e.tensor_mul(o, o, a),
                         waits=[g_done[(tc, et)]], incs=[(VE, 1)])
            st_rd[sl] = (VE, e2[VE])
        if tc < TC - 1:
            adds = []
            for et in range(ET):
                bk = banks[6 + et % 2]
                mm(oc_sb[:, :],
                   hid[:, kv_of + tc * D + et * EW:kv_of + tc * D + (et + 1) * EW],
                   bk, bk.t[0:1, 0:EW], start=True, stop=True)
                ad = evac("vector", VE,
                          lambda e, o=pfx[0:1, et * EW:(et + 1) * EW],
                                 s=bk.t[0:1, 0:EW]: e.tensor_add(o, o, s), bk)
                adds.append((VE, ad[VE]))
            pc_state[0] = prog.ev("scalar",
                                  lambda e: e.activation(pfx_bf[:, :],
                                                         pfx[0:1, :], AF.Identity),
                                  waits=adds, incs=[(SE, 1)])

    Q = DC * 128
    w13bufs = [(wbuf[:, 0:Q], wbuf[:, Q:2 * Q]),
               (wbuf[:, 2 * Q:3 * Q], wbuf[:, 3 * Q:4 * Q]),
               (aux[:, 0:Q], aux[:, Q:2 * Q]),
               (aux[:, 2 * Q:3 * Q], aux[:, 3 * Q:4 * Q])]
    w2fbufs = [wbuf[:, 0:AQ * DSL], wbuf[:, AQ * DSL:2 * AQ * DSL]]
    w2f_g_r = w2f_g.ap().rearrange("(r b a) d -> r b a d", b=128, a=HC)
    out_prev = None
    wbuf_free = list(qcons[0])
    aux_free = list(qcons[1])
    hid_w = {}
    w13cons = [[], [], [], []]
    aux_fin = [[], []]
    s_all = None

    def emit_w13(hf, al):
        sl = al % 4
        if w13cons[sl]:
            ws = list(w13cons[sl])
        elif sl < 2:
            ws = list(wbuf_free)
        else:
            ws = list(aux_free) + list(aux_fin[hf])
        l1 = dma("gpsimd", w13bufs[sl][0][:, :].rearrange("p (c j) -> p c j", c=DC),
                 w1s[al, :, :, :], GL[sl], waits=ws)
        l3 = dma("gpsimd", w13bufs[sl][1][:, :].rearrange("p (c j) -> p c j", c=DC),
                 w3s[al, :, :, :], GL[sl])
        bg = banks[0 + 2 * (al % 2)]
        bl = banks[1 + 2 * (al % 2)]
        for dcc in range(DC):
            mm(w13bufs[sl][0][:, dcc * 128:(dcc + 1) * 128],
               xnT[:, dcc * T + hf * TH:dcc * T + hf * TH + TH],
               bg, bg.t[:, 0:TH], start=(dcc == 0), stop=(dcc == DC - 1),
               waits=[(GL[sl], l3[GL[sl]])] if dcc == 0 else [])
        for dcc in range(DC):
            mm(w13bufs[sl][1][:, dcc * 128:(dcc + 1) * 128],
               xnT[:, dcc * T + hf * TH:dcc * T + hf * TH + TH],
               bl, bl.t[:, 0:TH], start=(dcc == 0), stop=(dcc == DC - 1),
               waits=[(GL[sl], l3[GL[sl]])] if dcc == 0 else [])
        w13cons[sl] = [(PE, PE.n)]
        ssl = al % 2
        sv = evac("scalar", SE,
                  lambda e, o=st[ssl][:, 0:TH], s=bg.t[:, 0:TH]:
                      e.activation(o, s, AF.Silu), bg,
                  waits=[st_rd[ssl]] if st_rd[ssl] else [])
        hw = evac("vector", VE,
                  lambda e, o=hid[:, al * TH:(al + 1) * TH],
                         s=bl.t[:, 0:TH], a=st[ssl][:, 0:TH]:
                      e.tensor_mul(o, a, s), bl,
                  waits=[(SE, sv[SE])])
        st_rd[ssl] = (VE, hw[VE])
        hid_w[al] = (VE, hw[VE])

    for hf in range(2):
        if hf == 1:
            aux_fin[1] = [(VE, VE.n)]
        if hf == 0:
            for tc in range(TC):
                emit_c_chunk(tc)
                for al in range(4 * tc, 4 * tc + 4):
                    if al < HC:
                        emit_w13(0, al)
            for al in range(4 * TC, HC):
                emit_w13(0, al)
            s_all = (VE, VE.n)
        else:
            for al in range(HC):
                emit_w13(1, al)
        pe_after_w13 = PE.n
        w2fcons = [w13cons[0][0] if w13cons[0] else (PE, pe_after_w13),
                   w13cons[1][0] if w13cons[1] else (PE, pe_after_w13)]
        for dp in range(NC):
            obanks = [banks[4 + 2 * (dp % 2) + dcb] for dcb in range(DPC)]
            for aq0 in range(0, HC, AQ):
                sl = (aq0 // AQ) % 2
                ws = [(CC1, 1), w2fcons[sl]]
                wl = dma("sync",
                         w2fbufs[sl][:, :].rearrange("p (a d) -> p a d", a=AQ),
                         w2f_g_r[dp, :, aq0:aq0 + AQ, :], SWS[sl], waits=ws)
                for ai in range(AQ):
                    al = aq0 + ai
                    for dcb in range(DPC):
                        mm(w2fbufs[sl][:, ai * DSL + dcb * 128:
                                       ai * DSL + (dcb + 1) * 128],
                           hid[:, al * TH:(al + 1) * TH],
                           obanks[dcb], obanks[dcb].t[:, 0:TH],
                           start=(al == 0), stop=(al == HC - 1),
                           waits=([(SWS[sl], wl[SWS[sl]]), hid_w[al]] if ai == 0
                                  else [hid_w[al]]))
                w2fcons[sl] = (PE, PE.n)
            for dcb in range(DPC):
                bk = obanks[dcb]
                ssl = dcb % 2
                sv = evac("scalar", SE,
                          lambda e, o=st[ssl][:, 0:TH], s=bk.t[:, 0:TH]:
                              e.activation(o, s, AF.Silu), bk,
                          waits=[st_rd[ssl]] if st_rd[ssl] else [])
                tb = banks[dcb % 2]
                views = transpose_grp(
                    [st[ssl][:, j * 128:(j + 1) * 128] for j in range(THC)],
                    tb, waits=[(SE, sv[SE])])
                st_rd[ssl] = (PE, PE.n)
                doff = (dp * DPC + dcb) * 128
                gdst = aux[:, :].rearrange("p (j d) -> p j d", d=D)[
                    :, 0:THC, doff:doff + 128]
                gsrc = tb.t[:, 0:THC * 64].bitcast(BF16).rearrange(
                    "p (j t) -> p j t", t=128)
                evac("vector", VE,
                     lambda e, o=gdst, s=gsrc: e.tensor_copy(o, s),
                     tb,
                     waits=aux_free if (hf == 0 and dp == 0 and dcb == 0)
                           else [])
            if dp == NC // 2 - 1:
                for j in range(THC):
                    tcc = hf * THC + j
                    prog.ev("vector",
                            lambda e, j=j, tcc=tcc: e.tensor_mul(
                                xn_st[:, j * (D // 2):(j + 1) * (D // 2)],
                                g_s[:, tcc * D:tcc * D + D // 2],
                                aux[:, j * D:j * D + D // 2]),
                            waits=[s_all], incs=[(VE, 1)])
        g2_evd = (VE, VE.n)
        wbuf_free = [(PE, PE.n)]
        w13cons = [[], [], [], []]
        HD = D // 2
        if hf == 1:
            fx = [x_st, wbuf[:, 0:4096].bitcast(F32), wbuf[:, 4096:8192].bitcast(F32)]
            fxw = [list(x_users) + list(wbuf_free), list(wbuf_free), list(wbuf_free)]
            fsx = [SXs[0], SXs[1], SXs[2]]
        else:
            fx = [x_st]
            fxw = [list(x_users)]
            fsx = [SXs[0]]
        nfx = len(fx)
        for j in range(THC):
            tc = hf * THC + j
            bi_ = j % nfx
            xb = fx[bi_]
            SX = fsx[bi_]
            xl = dma("sync", xb[:, :], x_tok[tc * 128:(tc + 1) * 128, :], SX,
                     waits=fxw[bi_])
            o1 = prog.ev("vector",
                         lambda e, j=j, xx=xb: e.scalar_tensor_tensor(
                             o_st[:, 0:HD], xn_st[:, j * HD:(j + 1) * HD], 1.0,
                             xx[:, 0:HD], op0=ALU.mult, op1=ALU.add),
                         waits=[(SX, xl[SX]), (SE, pc_state[0][SE])] +
                               ([out_prev] if out_prev else []),
                         incs=[(VE, 1)])
            prog.ev("vector",
                    lambda e, j=j, tcc=tc: e.tensor_mul(
                        xn_st[:, j * HD:(j + 1) * HD],
                        g_s[:, tcc * D + HD:(tcc + 1) * D],
                        aux[:, j * D + HD:(j + 1) * D]),
                    waits=[g2_evd], incs=[(VE, 1)])
            ou = prog.ev("vector",
                         lambda e, j=j, xx=xb: e.scalar_tensor_tensor(
                             o_st[:, HD:D], xn_st[:, j * HD:(j + 1) * HD], 1.0,
                             xx[:, HD:D], op0=ALU.mult, op1=ALU.add),
                         incs=[(VE, 1)])
            fxw[bi_] = [(VE, ou[VE])]
            x_users = [(VE, ou[VE])]
            so = dma("sync", out_tok[tc * 128:(tc + 1) * 128, :], o_st[:, :], FIN,
                     waits=[(VE, ou[VE])])
            out_prev = (FIN, so[FIN])

    prog.ev("gpsimd", lambda e: e.nop(), waits=[(FIN, FIN.n)])

    prog.realize(nc, ctx)
    ctx.close()
    return nc


def _hadamard(n):
    h = np.array([[1.0]])
    while h.shape[0] < n:
        h = np.block([[h, h], [h, -h]])
    return h


def _tile_qkvg(W, cfg):
    WT = np.asarray(W, np.float32).T
    A = WT.reshape(cfg.DC, 128, cfg.ET, cfg.EW)
    return np.ascontiguousarray(A.transpose(2, 1, 0, 3))


def _tile_w13(W, cfg):
    WT = np.asarray(W, np.float32).T
    A = WT.reshape(cfg.DC, 128, cfg.HC, 128)
    return np.ascontiguousarray(A.transpose(2, 1, 0, 3))


def make_inputs(cfg, x, Wq, bq, Wk, bk, Wv, bv, Wg, bg, W1, W2, W3):
    f32 = np.float32
    x2 = np.ascontiguousarray(np.asarray(x, f32).reshape(cfg.S, cfg.D))
    W2T = np.ascontiguousarray(np.asarray(W2, f32).T)
    tiles = {p: _tile_qkvg(w, cfg) for p, w in
             (("k", Wk), ("v", Wv), ("q", Wq), ("g", Wg))}
    w1t = _tile_w13(W1, cfg)
    w3t = _tile_w13(W3, cfg)
    bias4 = np.stack([np.asarray(b, f32) for b in (bk, bv, bq, bg)])
    u = np.triu(np.ones((128, 128))).astype(BF)
    ident = np.eye(128).astype(BF)
    hamat_a = (_hadamard(cfg.HC) / 2.0 ** cfg.a_exp).astype(BF)
    hamat_b = (_hadamard(128) / 2.0 ** cfg.b_exp).astype(BF)
    onesr = np.ones((1, 128)).astype(BF)
    onesc = np.ones((128, 1)).astype(BF)
    pmask = np.zeros((4, 4 * 128))
    for p in range(4):
        pmask[p, p * 128:(p + 1) * 128] = 1.0
    pmask = pmask.astype(BF)
    in_maps = []
    for c in range(cfg.NC):
        sl = np.ascontiguousarray(
            W2T[:, c * cfg.DSL:(c + 1) * cfg.DSL].reshape(cfg.HC, 128 * cfg.DSL))
        rm = np.zeros((cfg.NC, 1))
        rm[:c, 0] = 1.0
        in_maps.append(dict(
            x_tok=np.ascontiguousarray(x2[c * cfg.T:(c + 1) * cfg.T, :]),
            wks=tiles["k"], wvs=tiles["v"], wqs=tiles["q"], wgs=tiles["g"],
            bias4=bias4, w1s=w1t, w3s=w3t, w2ts=sl,
            u128=u, identb=ident, ha=hamat_a, hb=hamat_b,
            onesr=onesr, onesc=onesc, pmask=pmask, rmask=rm.astype(BF),
        ))
    return in_maps


def kernel(L0_attn_x, L0_attn_Wq, L0_attn_bq, L0_attn_Wk, L0_attn_bk,
           L0_attn_Wv, L0_attn_bv, L0_attn_Wg, L0_attn_bg,
           L0_ffn_W1, L0_ffn_W2, L0_ffn_W3):
    cfg = Cfg(S=8192, D=2048, H=8192, NC=8)
    in_maps = make_inputs(cfg, L0_attn_x, L0_attn_Wq, L0_attn_bq, L0_attn_Wk,
                          L0_attn_bk, L0_attn_Wv, L0_attn_bv, L0_attn_Wg,
                          L0_attn_bg, L0_ffn_W1, L0_ffn_W2, L0_ffn_W3)
    nc = build(cfg)
    res = run_bass_kernel_spmd(nc, in_maps, core_ids=list(range(cfg.NC)))
    out = np.concatenate([r["out_tok"] for r in res.results], axis=0)
    return out.reshape(1, cfg.S, cfg.D).astype(np.float32)
